# version 03
# baseline (speedup 1.0000x reference)
import numpy as np
import ml_dtypes
from contextlib import ExitStack

import concourse.bass as bass
import concourse.mybir as mybir
from concourse.bass_utils import run_bass_kernel_spmd

F32 = mybir.dt.float32
BF16 = mybir.dt.bfloat16
AF = mybir.ActivationFunctionType
ALU = mybir.AluOpType
AX = mybir.AxisListType.X
BF = ml_dtypes.bfloat16

D = 1024
CH = 4096
NCORES = 8
LEAD = 640
NEG = -1.0e30


class R:
    __slots__ = ("w", "rs")
    def __init__(self):
        self.w = None
        self.rs = []

ENGS = ("sp", "pool", "pe", "act", "dve")

class Sched:
    def __init__(self):
        self.ops = {n: [] for n in ENGS}
        self.cnt = {n: 0 for n in ENGS}
        self.bar = []

    def barrier(self):
        self.bar = [(n, self.cnt[n]) for n in ENGS if self.cnt[n] > 0]

    def op(self, eng, fn, reads=(), writes=(), dma=False):
        dd = list(self.bar)
        for r in reads:
            if r.w is not None:
                dd.append(r.w)
        for wr in writes:
            dd.extend(wr.rs)
            if wr.w is not None:
                dd.append(wr.w)
        inc = 16 if dma else 1
        self.cnt[eng] += inc
        tok = (eng, self.cnt[eng])
        self.ops[eng].append((fn, tuple(dd), inc))
        for r in reads:
            r.rs.append(tok)
        for wr in writes:
            wr.rs = []
            wr.w = tok
        return tok

    def replay(self, nc, sems):
        engo = {"sp": nc.sync, "pool": nc.gpsimd, "pe": nc.tensor,
                "act": nc.scalar, "dve": nc.vector}

        def emit(name):
            e = engo[name]
            waited = {}
            for fn, deps, inc in self.ops[name]:
                need = {}
                for (s, v) in deps:
                    if s == name and name in ("pe", "sp", "pool"):
                        continue
                    if v > need.get(s, -1):
                        need[s] = v
                for s, v in sorted(need.items()):
                    if waited.get(s, -1) >= v:
                        continue
                    e.wait_ge(sems[s], v)
                    waited[s] = v
                inst = fn()
                inst.then_inc(sems[name], inc)
            if name == "pool":
                e.wait_ge(sems["pool"], self.cnt["pool"])

        with nc.Block() as blk:
            @blk.sync
            def _(s):
                emit("sp")

            @blk.gpsimd
            def _(s):
                emit("pool")

            @blk.tensor
            def _(s):
                emit("pe")

            @blk.scalar
            def _(s):
                emit("act")

            @blk.vector
            def _(s):
                emit("dve")


def mk_sems(nc, es):
    sems = {}
    for n in ENGS:
        sems[n] = es.enter_context(nc.semaphore("sem_" + n))
    return sems



def ln_scratch(S, tag, nset=2):
    out = []
    for i in range(nset):
        t = {}
        t["xc"] = S("lnxc%s%d" % (tag, i), [128, D], F32)
        t["sq"] = S("lnsq%s%d" % (tag, i), [128, D], F32)
        t["m1"] = S("lnm1%s%d" % (tag, i), [128, 1], F32)
        t["m2"] = S("lnm2%s%d" % (tag, i), [128, 1], F32)
        t["ss"] = S("lnss%s%d" % (tag, i), [128, 1], F32)
        t["sd"] = S("lnsd%s%d" % (tag, i), [128, 1], F32)
        t["rs"] = S("lnrs%s%d" % (tag, i), [128, 1], F32)
        t["R"] = R()
        out.append(t)
    return out


def emit_ln(sch, nc, scr, src_r, src_ap, dst_r, dst_ap, P=128):
    t = scr
    rr = t["R"]
    sch.op("dve", lambda: nc.vector.reduce_sum(t["m1"][:P, :], src_ap, axis=AX),
           reads=[src_r], writes=[rr])
    sch.op("dve", lambda: nc.vector.tensor_scalar_mul(t["m2"][:P, :], t["m1"][:P, :], -1.0 / D),
           reads=[rr], writes=[rr])
    sch.op("dve", lambda: nc.vector.tensor_scalar_add(t["xc"][:P, :], src_ap, t["m2"][:P, :]),
           reads=[src_r, rr], writes=[rr])
    sch.op("act", lambda: nc.scalar.activation(t["sq"][:P, :], t["xc"][:P, :], AF.Square,
                                               accum_out=t["ss"][:P, :]),
           reads=[rr], writes=[rr])
    sch.op("dve", lambda: nc.vector.tensor_scalar(t["m1"][:P, :], t["ss"][:P, :], 1.0 / D, 1e-5,
                                                  ALU.mult, ALU.add),
           reads=[rr], writes=[rr])
    sch.op("act", lambda: nc.scalar.activation(t["sd"][:P, :], t["m1"][:P, :], AF.Sqrt),
           reads=[rr], writes=[rr])
    sch.op("dve", lambda: nc.vector.reciprocal(t["rs"][:P, :], t["sd"][:P, :]),
           reads=[rr], writes=[rr])
    sch.op("dve", lambda: nc.vector.tensor_scalar_mul(dst_ap, t["xc"][:P, :], t["rs"][:P, :]),
           reads=[rr], writes=[dst_r])


def emit_transp8(sch, nc, pT, pT_r, ident, ident_r, src_r, src_ap_fn, dst_r, dst_ap_fn, np_=8):
    nb = len(pT)
    for p in range(np_):
        b = p % nb
        sch.op("pe", lambda p=p, b=b: nc.tensor.transpose(pT[b][:, :], src_ap_fn(p), ident[:, :]),
               reads=[src_r, ident_r], writes=[pT_r[b]])
        sch.op("act", lambda p=p, b=b: nc.scalar.copy(dst_ap_fn(p), pT[b][:, :]),
               reads=[pT_r[b]], writes=[dst_r])


def emit_wload(sch, nc, wst, wst_r, dram_ap, rows0, ncols, dst_ap, dst_r, scale=None, scale_r=None):
    sch.op("sp", lambda: nc.sync.dma_start(out=wst[:, 0:ncols], in_=dram_ap[rows0:rows0 + 128, :]),
           reads=[], writes=[wst_r], dma=True)
    if scale is None:
        sch.op("act", lambda: nc.scalar.copy(dst_ap, wst[:, 0:ncols]),
               reads=[wst_r], writes=[dst_r])
    else:
        sch.op("dve", lambda: nc.vector.tensor_scalar_mul(dst_ap, wst[:, 0:ncols], scale),
               reads=[wst_r, scale_r], writes=[dst_r])



def emit_A(nc, sch, dr, out_ap, out_off, pref="A"):
    xa = dr["xa"]
    w1d = dr["aw1d"]
    w2d = dr["aw2d"]
    biasr = dr["abiasr"]
    etT = dr["etT"]
    idb = dr["idb"]
    on1f = dr["on1f"]

    es = ExitStack()
    S = lambda n, sh, dt: es.enter_context(nc.sbuf_tensor(pref + n, sh, dt))
    P = lambda n, sh, dt: es.enter_context(nc.psum_tensor(pref + n, sh, dt))

    w1b = S("w1b", [128, 8 * D], BF16)
    w2b = S("w2b", [128, 8 * D], BF16)
    wst = S("wst", [128, D], F32)
    etsb = S("etsb", [128, 8 * 21], F32)
    ident = S("ident", [128, 128], BF16)
    ones1 = S("ones1", [1, 128], F32)
    biasbc = S("biasbc", [128, D], F32)
    biasrow = S("biasrow", [1, D], F32)
    xt = [S("xt%d" % i, [128, D], F32) for i in range(2)]
    xr = [S("xr%d" % i, [128, D], F32) for i in range(2)]
    xnb = [S("xnb%d" % i, [128, D], BF16) for i in range(2)]
    xfm = [S("xfm%d" % p, [128, 4224], BF16) for p in range(8)]
    hc = [S("hc%d" % i, [128, 1024], F32) for i in range(2)]
    hcb = [S("hcb%d" % p, [128, 1024], BF16) for p in range(8)]
    ysb = [S("ysb%d" % i, [128, D], F32) for i in range(2)]
    scr = ln_scratch(S, "a")

    pT = [P("pT%d" % i, [128, 128], BF16) for i in range(2)]
    py = [P("py%d" % i, [128, D], F32) for i in range(2)]
    pb = P("pb", [128, 512], F32)

    w1b_r, w2b_r, wst_r, etsb_r, ident_r, ones1_r = R(), R(), R(), R(), R(), R()
    biasbc_r, biasrow_r = R(), R()
    xt_r = [R(), R()]
    xr_r = [R(), R()]
    xnb_r = [R(), R()]
    xfm_r = [R() for _ in range(8)]
    hc_r = [R(), R()]
    hcb_r = [R() for _ in range(8)]
    ysb_r = [R(), R()]
    pT_r = [R(), R()]
    py_r = [R(), R()]
    pb_r = R()
    out_r = R()

    sch.op("sp", lambda: nc.sync.dma_start(out=ident[:, :], in_=idb[:, :]), writes=[ident_r], dma=True)
    sch.op("sp", lambda: nc.sync.dma_start(out=ones1[:, :], in_=on1f[:, :]), writes=[ones1_r], dma=True)
    sch.op("sp", lambda: nc.sync.dma_start(out=biasrow[:, :], in_=biasr[:, :]), writes=[biasrow_r], dma=True)
    for p in range(8):
        sch.op("sp", lambda p=p: nc.sync.dma_start(out=etsb[:, p * 21:(p + 1) * 21],
                                                   in_=etT[p * 128:(p + 1) * 128, :]),
               writes=[etsb_r], dma=True)
    for cg in range(2):
        sch.op("pe", lambda cg=cg: nc.tensor.matmul(pb[:, :], ones1[:, :],
                                                    biasrow[:, cg * 512:(cg + 1) * 512],
                                                    start=True, stop=True),
               reads=[ones1_r, biasrow_r], writes=[pb_r])
        sch.op("act", lambda cg=cg: nc.scalar.copy(biasbc[:, cg * 512:(cg + 1) * 512], pb[:, :]),
               reads=[pb_r], writes=[biasbc_r])
    for k in range(8):
        emit_wload(sch, nc, wst, wst_r, w1d, 128 * k, D, w1b[:, k * D:(k + 1) * D], w1b_r)
        emit_wload(sch, nc, wst, wst_r, w2d, 128 * k, D, w2b[:, k * D:(k + 1) * D], w2b_r)

    for t in range(33):
        i = t % 2
        sch.op("sp", lambda t=t, i=i: nc.sync.dma_start(out=xt[i][:, :], in_=xa[128 * t:128 * (t + 1), :]),
               writes=[xt_r[i]], dma=True)
        emit_ln(sch, nc, scr[i], xt_r[i], xt[i][:, :], xnb_r[i], xnb[i][:, :])
        emit_transp8(sch, nc, pT, pT_r, ident, ident_r, xnb_r[i],
                     lambda p, i=i: xnb[i][:, p * 128:(p + 1) * 128],
                     xfm_r[0],
                     lambda p, t=t: xfm[p][:, 128 * t:128 * (t + 1)])
    for p in range(1, 8):
        xfm_r[p] = xfm_r[0]

    for c in range(4):
        c0 = 128 + 1024 * c
        for p in range(8):
            for j in range(1, 21):
                dst = hc[j % 2]
                dst_r = hc_r[j % 2]
                src_r = hc_r[(j - 1) % 2]
                if j == 1:
                    sch.op("dve", lambda p=p, c0=c0, j=j, dst=dst:
                           nc.vector.tensor_scalar_mul(dst[:, :], xfm[p][:, c0 - j:c0 - j + 1024],
                                                       etsb[:, p * 21 + j:p * 21 + j + 1]),
                           reads=[xfm_r[0], etsb_r], writes=[dst_r])
                else:
                    sch.op("dve", lambda p=p, c0=c0, j=j, dst=dst, src=hc[(j - 1) % 2]:
                           nc.vector.scalar_tensor_tensor(dst[:, :], xfm[p][:, c0 - j:c0 - j + 1024],
                                                          etsb[:, p * 21 + j:p * 21 + j + 1],
                                                          src[:, :], ALU.mult, ALU.add),
                           reads=[xfm_r[0], etsb_r, src_r], writes=[dst_r])
            sch.op("act", lambda p=p: nc.scalar.copy(hcb[p][:, :], hc[0][:, :]),
                   reads=[hc_r[0]], writes=[hcb_r[p]])
        for t in range(8 * c + 1, 8 * c + 9):
            i = t % 2
            oi = t - 1
            ic = oi - 8 * c
            sch.op("sp", lambda t=t, i=i: nc.sync.dma_start(out=xr[i][:, :], in_=xa[128 * t:128 * (t + 1), :]),
                   writes=[xr_r[i]], dma=True)
            for cg in range(2):
                for k in range(8):
                    sch.op("pe", lambda t=t, i=i, cg=cg, k=k:
                           nc.tensor.matmul(py[i][:, cg * 512:(cg + 1) * 512],
                                            xfm[k][:, 128 * t:128 * (t + 1)],
                                            w1b[:, k * D + cg * 512:k * D + cg * 512 + 512],
                                            start=(k == 0), stop=False),
                           reads=[xfm_r[0], w1b_r], writes=[py_r[i]])
                for k in range(8):
                    sch.op("pe", lambda i=i, cg=cg, k=k, ic=ic:
                           nc.tensor.matmul(py[i][:, cg * 512:(cg + 1) * 512],
                                            hcb[k][:, 128 * ic:128 * (ic + 1)],
                                            w2b[:, k * D + cg * 512:k * D + cg * 512 + 512],
                                            start=False, stop=(k == 7)),
                           reads=[hcb_r[k], w2b_r], writes=[py_r[i]])
            sch.op("dve", lambda i=i: nc.vector.tensor_add(ysb[i][:, :], py[i][:, :], xr[i][:, :]),
                   reads=[py_r[i], xr_r[i]], writes=[ysb_r[i]])
            sch.op("dve", lambda i=i: nc.vector.tensor_add(xr[i][:, :], ysb[i][:, :], biasbc[:, :]),
                   reads=[ysb_r[i], biasbc_r], writes=[xr_r[i]])
            sch.op("pool", lambda i=i, oi=oi: nc.gpsimd.dma_start(
                out=out_ap[out_off + 128 * oi:out_off + 128 * (oi + 1), :], in_=xr[i][:, :]),
                   reads=[xr_r[i]], writes=[out_r], dma=True)

    es.close()



def emit_B(nc, sch, dr, lw, pre_ln, final_add, xb, out_ap, out_off, xres=None, xres_off=0, pref="B"):
    wqkvd = lw["wqkvd"]
    wod = lw["wod"]
    w1ad = lw["w1ad"]
    w1gd = lw["w1gd"]
    w2d = lw["w2d"]
    b1ad = lw["b1ad"]
    b1gd = lw["b1gd"]
    b2d = lw["b2d"]
    lngd = lw["lngd"]
    lnbd = lw["lnbd"]
    qs8d = lw["qs8d"]
    ks8d = lw["ks8d"]
    cq8 = dr["cq8"]
    sq8 = dr["sq8"]
    ck8 = dr["ck8"]
    sk8 = dr["sk8"]
    bmd = dr["bmd"]
    kvd = dr["kvd"]
    idb = dr["idb"]
    on1f = dr["on1f"]
    on1b = dr["on1b"]
    xresd = xres
    yo_ap = out_ap
    mscr = dr["mscr"]
    ypart = dr["ypart"]
    uscr = dr["uscr"]

    es = ExitStack()
    S = lambda n, sh, dt: es.enter_context(nc.sbuf_tensor(pref + n, sh, dt))
    P = lambda n, sh, dt: es.enter_context(nc.psum_tensor(pref + n, sh, dt))

    wbig = S("wbig", [128, 33792], BF16)
    wst = S("wstb", [128, 1536], F32)
    ident = S("identb", [128, 128], BF16)
    ones1 = S("ones1b", [1, 128], F32)
    ones1bf = S("ones1bf", [1, 128], BF16)
    lngc = S("lngc", [128, 8], F32)
    lnbc = S("lnbc", [128, 8], F32)
    beffa = S("beffa", [1, 2816], BF16)
    beffg = S("beffg", [1, 2816], BF16)
    b2bc = S("b2bc", [128, D], F32)
    rowst = S("rowst", [1, 2048], F32)
    qsbc = S("qsbc", [128, 512], F32)
    ksbc = S("ksbc", [128, 512], F32)
    kvrow = S("kvrow", [1, 4736], BF16)
    bm = S("bm", [128, 640], F32)

    hs = [S("hs%d" % i, [128, D], F32) for i in range(2)]
    hbb0 = S("hbb0", [128, D], BF16)
    hbb = [hbb0, hbb0]
    hfm0 = S("hfm0", [128, D], BF16)
    hfm = [hfm0, hfm0]
    tabs = [S("tab%d" % i, [128, 512], F32) for i in range(4)]
    qn = S("qn", [128, 512], F32)
    qsh = S("qsh", [128, 512], F32)
    qsh2 = S("qsh2", [128, 512], F32)
    qtmp = S("qtmp", [128, 512], F32)
    qrot = S("qrot", [128, 512], BF16)
    junk64 = S("junk64", [128, 64], F32)
    qss = S("qss", [128, 8], F32)
    qsm = S("qsm", [128, 16], F32)
    qrc = S("qrc", [128, 8], F32)
    kn = S("kn", [128, 512], F32)
    ksh = S("ksh", [128, 512], F32)
    ksh2 = S("ksh2", [128, 512], F32)
    ktmp = S("ktmp", [128, 512], F32)
    krot = S("krot", [128, 512], BF16)
    kss = S("kss", [128, 8], F32)
    ksm = S("ksm", [128, 16], F32)
    krc = S("krc", [128, 8], F32)
    qfm = S("qfm", [128, 512], BF16)
    kfm = [S("kfm%d" % s, [128, 512], BF16) for s in range(5)]
    vbs = [S("vbs%d" % s, [128, 512], BF16) for s in range(5)]
    pbf = S("pbf", [128, 640], F32)
    pmx = S("pmx", [128, 1], F32)
    pnm = S("pnm", [128, 1], F32)
    psm = S("psm", [128, 1], F32)
    prc = S("prc", [128, 1], F32)
    pbn = S("pbn", [128, 640], BF16)
    pbn2 = S("pbn2", [128, 640], BF16)
    pts = S("pts", [128, 640], BF16)
    ofm = S("ofm", [128, 512], BF16)
    msb0 = S("msb0", [128, D], F32)
    msb = [msb0, msb0]
    h2n0 = S("h2n0", [128, D], BF16)
    h2n = [h2n0, h2n0]
    ggb = S("ggb", [128, 512], F32)
    prod = S("prod", [128, 512], BF16)
    hidT = S("hidT", [128, 1408], BF16)
    yps0 = S("yps0", [128, D], F32)
    yps = [yps0, yps0]
    xrs0 = S("xrs0", [128, D], F32)
    xrs = [xrs0, xrs0]
    _scr = ln_scratch(S, "b", 1)
    scr = [_scr[0], _scr[0]]
    scr1 = _scr[0]
    urow = rowst[0:1, 0:D]
    urowo = rowst[0:1, D:2 * D]

    p512a = P("p512a", [128, 512], F32)
    p512b = P("p512b", [128, 512], F32)
    psim = P("psim", [128, 640], F32)
    pT = [P("pTb0", [128, 128], BF16)]
    po = P("po", [128, 128], F32)
    pm = P("pm", [128, D], F32)

    wbig_r, wst_r, ident_r, ones1_r, ones1bf_r = R(), R(), R(), R(), R()
    lngc_r, lnbc_r, beffa_r, beffg_r, b2bc_r, rowst_r = R(), R(), R(), R(), R(), R()
    qsbc_r, ksbc_r, kvrow_r, bm_r = R(), R(), R(), R()
    hs_r = [R(), R()]
    _hbb_r = R()
    hbb_r = [_hbb_r, _hbb_r]
    _hfm_r = R()
    hfm_r = [_hfm_r, _hfm_r]
    tabs_r = [R() for _ in range(4)]
    qn_r, qsh_r, qsh2_r, qtmp_r, qrot_r, junk64_r, qss_r, qsm_r, qrc_r = R(), R(), R(), R(), R(), R(), R(), R(), R()
    kn_r, ksh_r, ksh2_r, ktmp_r, krot_r, kss_r, ksm_r, krc_r = R(), R(), R(), R(), R(), R(), R(), R()
    qfm_r = R()
    kfm_r = [R() for _ in range(5)]
    vbs_r = [R() for _ in range(5)]
    pbf_r, pmx_r, pnm_r, psm_r, prc_r, pbn_r, pbn2_r, pts_r, ofm_r = R(), R(), R(), R(), R(), R(), R(), R(), R()
    _msb_r = R()
    msb_r = [_msb_r, _msb_r]
    _h2n_r = R()
    h2n_r = [_h2n_r, _h2n_r]
    ggb_r, prod_r, hidT_r = R(), R(), R()
    _yps_r = R()
    yps_r = [_yps_r, _yps_r]
    _xrs_r = R()
    xrs_r = [_xrs_r, _xrs_r]
    p512a_r, p512b_r, psim_r, po_r, pm_r = R(), R(), R(), R(), R()
    pT_r = [R()]
    mscr_r, ypart_r, uscr_r, out_r = R(), R(), R(), R()

    sch.op("sp", lambda: nc.sync.dma_start(out=ident[:, :], in_=idb[:, :]), writes=[ident_r], dma=True)
    sch.op("sp", lambda: nc.sync.dma_start(out=ones1[:, :], in_=on1f[:, :]), writes=[ones1_r], dma=True)
    sch.op("sp", lambda: nc.sync.dma_start(out=ones1bf[:, :], in_=on1b[:, :]), writes=[ones1bf_r], dma=True)
    sch.op("sp", lambda: nc.sync.dma_start(out=kvrow[:, :], in_=kvd[:, :]), writes=[kvrow_r], dma=True)
    sch.op("sp", lambda: nc.sync.dma_start(out=bm[:, :], in_=bmd[:, :]), writes=[bm_r], dma=True)
    for p in range(8):
        sch.op("sp", lambda p=p: nc.sync.dma_start(out=lngc[:, p:p + 1], in_=lngd[128 * p:128 * (p + 1), :]),
               writes=[lngc_r], dma=True)
        sch.op("sp", lambda p=p: nc.sync.dma_start(out=lnbc[:, p:p + 1], in_=lnbd[128 * p:128 * (p + 1), :]),
               writes=[lnbc_r], dma=True)
    for dst, dst_r, src in ((qsbc, qsbc_r, qs8d), (ksbc, ksbc_r, ks8d)):
        sch.op("sp", lambda src=src: nc.sync.dma_start(out=rowst[:, 0:512], in_=src[:, :]),
               writes=[rowst_r], dma=True)
        sch.op("pe", lambda: nc.tensor.matmul(p512a[:, :], ones1[:, :], rowst[:, 0:512],
                                              start=True, stop=True),
               reads=[ones1_r, rowst_r], writes=[p512a_r])
        sch.op("act", lambda dst=dst: nc.scalar.copy(dst[:, :], p512a[:, :]),
               reads=[p512a_r], writes=[dst_r])
    sch.op("sp", lambda: nc.sync.dma_start(out=rowst[:, 0:D], in_=b2d[:, :]), writes=[rowst_r], dma=True)
    for cg in range(2):
        sch.op("pe", lambda cg=cg: nc.tensor.matmul(p512a[:, :], ones1[:, :],
                                                    rowst[:, cg * 512:(cg + 1) * 512],
                                                    start=True, stop=True),
               reads=[ones1_r, rowst_r], writes=[p512a_r])
        sch.op("act", lambda cg=cg: nc.scalar.copy(b2bc[:, cg * 512:(cg + 1) * 512], p512a[:, :]),
               reads=[p512a_r], writes=[b2bc_r])

    for k in range(8):
        emit_wload(sch, nc, wst, wst_r, wqkvd, 128 * k, 1536,
                   wbig[:, k * 1536:(k + 1) * 1536], wbig_r)
    for k in range(4):
        emit_wload(sch, nc, wst, wst_r, wod, 128 * k, D,
                   wbig[:, 12288 + k * D:12288 + (k + 1) * D], wbig_r)

    if pre_ln:
        SRC = uscr
        SRC_r = uscr_r
        sch.op("sp", lambda: nc.sync.dma_start(out=urow, in_=xb[0:1, :]),
               writes=[rowst_r], dma=True)
        emit_ln(sch, nc, scr1, rowst_r, urow, rowst_r, urowo, P=1)
        sch.op("pool", lambda: nc.gpsimd.dma_start(out=uscr[0:1, :], in_=urowo),
               reads=[rowst_r], writes=[uscr_r], dma=True)
        for t in range(37):
            i = t % 2
            sch.op("sp", lambda t=t, i=i: nc.sync.dma_start(out=hs[i][:, :],
                                                            in_=xb[1 + 128 * t:1 + 128 * (t + 1), :]),
                   writes=[hs_r[i]], dma=True)
            emit_ln(sch, nc, scr[i], hs_r[i], hs[i][:, :], msb_r[i], msb[i][:, :])
            sch.op("pool", lambda t=t, i=i: nc.gpsimd.dma_start(out=uscr[1 + 128 * t:1 + 128 * (t + 1), :],
                                                                in_=msb[i][:, :]),
                   reads=[msb_r[i]], writes=[uscr_r], dma=True)
    else:
        SRC = xb
        SRC_r = R()

    for t in range(37):
        i = t % 2
        slot = t % 5
        sch.op("sp", lambda t=t, i=i: nc.sync.dma_start(out=hs[i][:, 0:512],
                                                        in_=SRC[1 + 128 * t:1 + 128 * (t + 1), 0:512]),
               reads=[SRC_r], writes=[hs_r[i]], dma=True)
        sch.op("sp", lambda t=t, i=i: nc.sync.dma_start(out=hs[i][:, 512:D],
                                                        in_=SRC[128 * t:128 * (t + 1), 512:D]),
               reads=[SRC_r], writes=[hs_r[i]], dma=True)
        sch.op("act", lambda i=i: nc.scalar.copy(hbb[i][:, :], hs[i][:, :]),
               reads=[hs_r[i]], writes=[hbb_r[i]])
        emit_transp8(sch, nc, pT, pT_r, ident, ident_r, hbb_r[i],
                     lambda p, i=i: hbb[i][:, p * 128:(p + 1) * 128],
                     hfm_r[i],
                     lambda p, i=i: hfm[i][:, p * 128:(p + 1) * 128])
        for k in range(8):
            sch.op("pe", lambda i=i, k=k: nc.tensor.matmul(p512a[:, :], hfm[i][:, k * 128:(k + 1) * 128],
                                                           wbig[:, k * 1536 + 512:k * 1536 + 1024],
                                                           start=(k == 0), stop=(k == 7)),
                   reads=[hfm_r[i], wbig_r], writes=[p512a_r])
        for k in range(8):
            sch.op("pe", lambda i=i, k=k: nc.tensor.matmul(p512b[:, :], hfm[i][:, k * 128:(k + 1) * 128],
                                                           wbig[:, k * 1536 + 1024:k * 1536 + 1536],
                                                           start=(k == 0), stop=(k == 7)),
                   reads=[hfm_r[i], wbig_r], writes=[p512b_r])
        for k in range(8):
            sch.op("pe", lambda i=i, k=k: nc.tensor.matmul(psim[:, 0:512], hfm[i][:, k * 128:(k + 1) * 128],
                                                           wbig[:, k * 1536:k * 1536 + 512],
                                                           start=(k == 0), stop=(k == 7)),
                   reads=[hfm_r[i], wbig_r], writes=[psim_r])
        sch.op("act", lambda slot=slot: nc.scalar.copy(vbs[slot][:, :], p512b[:, :]),
               reads=[p512b_r], writes=[vbs_r[slot]])
        for j, tb in enumerate((cq8, sq8, ck8, sk8)):
            sch.op("sp", lambda t=t, j=j, tb=tb: nc.sync.dma_start(out=tabs[j][:, :],
                                                                   in_=tb[128 * t:128 * (t + 1), :]),
                   writes=[tabs_r[j]], dma=True)
        for h in range(8):
            sch.op("act", lambda h=h: nc.scalar.activation(junk64[:, :], p512a[:, h * 64:(h + 1) * 64],
                                                           AF.Square, accum_out=kss[:, h:h + 1]),
                   reads=[p512a_r], writes=[junk64_r, kss_r])
        sch.op("act", lambda: nc.scalar.activation(ksm[:, 0:8], kss[:, :], AF.Sqrt),
               reads=[kss_r], writes=[ksm_r])
        sch.op("dve", lambda: nc.vector.tensor_scalar_max(ksm[:, 8:16], ksm[:, 0:8], 1e-12),
               reads=[ksm_r], writes=[ksm_r])
        sch.op("dve", lambda: nc.vector.reciprocal(krc[:, :], ksm[:, 8:16]),
               reads=[ksm_r], writes=[krc_r])
        for h in range(8):
            sch.op("dve", lambda h=h: nc.vector.tensor_scalar_mul(kn[:, h * 64:(h + 1) * 64],
                                                                  p512a[:, h * 64:(h + 1) * 64],
                                                                  krc[:, h:h + 1]),
                   reads=[p512a_r, krc_r], writes=[kn_r])
        sch.op("dve", lambda: nc.vector.tensor_mul(ktmp[:, :], kn[:, :], ksbc[:, :]),
               reads=[kn_r, ksbc_r], writes=[ktmp_r])
        for j in range(8):
            sch.op("pool", lambda j=j: nc.gpsimd.tensor_copy(ksh[:, j * 64:j * 64 + 32], ktmp[:, j * 64 + 32:j * 64 + 64]),
                   reads=[ktmp_r], writes=[ksh_r])
            sch.op("pool", lambda j=j: nc.gpsimd.tensor_copy(ksh[:, j * 64 + 32:j * 64 + 64], ktmp[:, j * 64:j * 64 + 32]),
                   reads=[ktmp_r], writes=[ksh_r])
        sch.op("dve", lambda: nc.vector.tensor_mul(kn[:, :], ktmp[:, :], tabs[2][:, :]),
               reads=[ktmp_r, tabs_r[2]], writes=[kn_r])
        sch.op("dve", lambda: nc.vector.tensor_mul(ksh2[:, :], ksh[:, :], tabs[3][:, :]),
               reads=[ksh_r, tabs_r[3]], writes=[ksh2_r])
        sch.op("dve", lambda: nc.vector.tensor_add(krot[:, :], kn[:, :], ksh2[:, :]),
               reads=[kn_r, ksh2_r], writes=[krot_r])
        for h in range(8):
            sch.op("act", lambda h=h: nc.scalar.activation(junk64[:, :], psim[:, h * 64:(h + 1) * 64],
                                                           AF.Square, accum_out=qss[:, h:h + 1]),
                   reads=[psim_r], writes=[junk64_r, qss_r])
        sch.op("act", lambda: nc.scalar.activation(qsm[:, 0:8], qss[:, :], AF.Sqrt),
               reads=[qss_r], writes=[qsm_r])
        sch.op("dve", lambda: nc.vector.tensor_scalar_max(qsm[:, 8:16], qsm[:, 0:8], 1e-12),
               reads=[qsm_r], writes=[qsm_r])
        sch.op("dve", lambda: nc.vector.reciprocal(qrc[:, :], qsm[:, 8:16]),
               reads=[qsm_r], writes=[qrc_r])
        for h in range(8):
            sch.op("dve", lambda h=h: nc.vector.tensor_scalar_mul(qn[:, h * 64:(h + 1) * 64],
                                                                  psim[:, h * 64:(h + 1) * 64],
                                                                  qrc[:, h:h + 1]),
                   reads=[psim_r, qrc_r], writes=[qn_r])
        sch.op("dve", lambda: nc.vector.tensor_mul(qtmp[:, :], qn[:, :], qsbc[:, :]),
               reads=[qn_r, qsbc_r], writes=[qtmp_r])
        for j in range(8):
            sch.op("pool", lambda j=j: nc.gpsimd.tensor_copy(qsh[:, j * 64:j * 64 + 32], qtmp[:, j * 64 + 32:j * 64 + 64]),
                   reads=[qtmp_r], writes=[qsh_r])
            sch.op("pool", lambda j=j: nc.gpsimd.tensor_copy(qsh[:, j * 64 + 32:j * 64 + 64], qtmp[:, j * 64:j * 64 + 32]),
                   reads=[qtmp_r], writes=[qsh_r])
        sch.op("dve", lambda: nc.vector.tensor_mul(qn[:, :], qtmp[:, :], tabs[0][:, :]),
               reads=[qtmp_r, tabs_r[0]], writes=[qn_r])
        sch.op("dve", lambda: nc.vector.tensor_mul(qsh2[:, :], qsh[:, :], tabs[1][:, :]),
               reads=[qsh_r, tabs_r[1]], writes=[qsh2_r])
        sch.op("dve", lambda: nc.vector.tensor_add(qrot[:, :], qn[:, :], qsh2[:, :]),
               reads=[qn_r, qsh2_r], writes=[qrot_r])
        emit_transp8(sch, nc, pT, pT_r, ident, ident_r, krot_r,
                     lambda p: krot[:, p * 128:(p + 1) * 128],
                     kfm_r[slot],
                     lambda p, slot=slot: kfm[slot][:, p * 128:(p + 1) * 128], np_=4)
        emit_transp8(sch, nc, pT, pT_r, ident, ident_r, qrot_r,
                     lambda p: qrot[:, p * 128:(p + 1) * 128],
                     qfm_r,
                     lambda p: qfm[:, p * 128:(p + 1) * 128], np_=4)

        if t < 4:
            continue
        for h in range(8):
            hp, dh = h // 2, h % 2
            for kt in range(5):
                ks_ = (t - 4 + kt) % 5
                col = kt * 128
                sch.op("pe", lambda t=t, kt=kt, col=col:
                       nc.tensor.matmul(psim[:, col:col + 128], ones1bf[:, :],
                                        kvrow[:, 128 * (t - 4 + kt):128 * (t - 4 + kt) + 128],
                                        start=True, stop=False),
                       reads=[ones1bf_r, kvrow_r], writes=[psim_r])
                sch.op("pe", lambda hp=hp, dh=dh, kt=kt, col=col, ks_=ks_:
                       nc.tensor.matmul(psim[:, col:col + 128],
                                        qfm[64 * dh:64 * dh + 64, hp * 128:(hp + 1) * 128],
                                        kfm[ks_][64 * dh:64 * dh + 64, hp * 128:(hp + 1) * 128],
                                        start=False, stop=True),
                       reads=[qfm_r, kfm_r[ks_]], writes=[psim_r])
            sch.op("dve", lambda: nc.vector.tensor_add(pbf[:, :], psim[:, :], bm[:, :]),
                   reads=[psim_r, bm_r], writes=[pbf_r])
            sch.op("dve", lambda: nc.vector.reduce_max(pmx[:, :], pbf[:, :], axis=AX),
                   reads=[pbf_r], writes=[pmx_r])
            sch.op("dve", lambda: nc.vector.tensor_scalar_mul(pnm[:, :], pmx[:, :], -1.0),
                   reads=[pmx_r], writes=[pnm_r])
            sch.op("act", lambda: nc.scalar.activation(pbn[:, :], pbf[:, :], AF.Exp,
                                                       bias=pnm[:, :], accum_out=psm[:, :]),
                   reads=[pbf_r, pnm_r], writes=[pbn_r, psm_r])
            sch.op("dve", lambda: nc.vector.reciprocal(prc[:, :], psm[:, :]),
                   reads=[psm_r], writes=[prc_r])
            sch.op("dve", lambda: nc.vector.tensor_scalar_mul(pbn2[:, :], pbn[:, :], prc[:, :]),
                   reads=[pbn_r, prc_r], writes=[pbn2_r])
            for kt in range(5):
                sch.op("pe", lambda kt=kt: nc.tensor.transpose(pT[0][:, :], pbn2[:, kt * 128:(kt + 1) * 128],
                                                               ident[:, :]),
                       reads=[pbn2_r, ident_r], writes=[pT_r[0]])
                sch.op("act", lambda kt=kt: nc.scalar.copy(pts[:, kt * 128:(kt + 1) * 128], pT[0][:, :]),
                       reads=[pT_r[0]], writes=[pts_r])
            for kt in range(5):
                ks_ = (t - 4 + kt) % 5
                sch.op("pe", lambda h=h, dh=dh, kt=kt, ks_=ks_:
                       nc.tensor.matmul(po[64 * dh:64 * dh + 64, :],
                                        vbs[ks_][:, h * 64:(h + 1) * 64],
                                        pts[:, kt * 128:(kt + 1) * 128],
                                        start=(kt == 0), stop=(kt == 4)),
                       reads=[vbs_r[ks_], pts_r], writes=[po_r])
            if dh == 1:
                sch.op("act", lambda hp=hp: nc.scalar.copy(ofm[:, hp * 128:(hp + 1) * 128], po[:, :]),
                       reads=[po_r], writes=[ofm_r])
        sch.op("sp", lambda t=t, i=i: nc.sync.dma_start(out=xrs[i][:, :],
                                                        in_=SRC[1 + 128 * t:1 + 128 * (t + 1), :]),
               reads=[SRC_r], writes=[xrs_r[i]], dma=True)
        for cg in range(2):
            for k in range(4):
                sch.op("pe", lambda k=k, cg=cg: nc.tensor.matmul(pm[:, cg * 512:(cg + 1) * 512],
                                                                 ofm[:, k * 128:(k + 1) * 128],
                                                                 wbig[:, 12288 + k * D + cg * 512:12288 + k * D + cg * 512 + 512],
                                                                 start=(k == 0), stop=(k == 3)),
                       reads=[ofm_r, wbig_r], writes=[pm_r])
        sch.op("dve", lambda i=i: nc.vector.tensor_add(msb[i][:, :], pm[:, :], xrs[i][:, :]),
               reads=[pm_r, xrs_r[i]], writes=[msb_r[i]])
        sch.op("pool", lambda t=t, i=i: nc.gpsimd.dma_start(out=mscr[1 + 128 * t:1 + 128 * (t + 1), :],
                                                            in_=msb[i][:, :]),
               reads=[msb_r[i]], writes=[mscr_r], dma=True)

    def load_half(half):
        hc0 = 1408 * half
        for which, wd, base in ((0, w1ad, 0), (1, w1gd, 11264)):
            pdst = [(p512a, p512a_r, 512), (p512b, p512b_r, 512), (psim, psim_r, 384)]
            for k in range(8):
                sch.op("sp", lambda k=k, wd=wd: nc.sync.dma_start(out=wst[:, 0:1408],
                                                                  in_=wd[128 * k:128 * (k + 1), hc0:hc0 + 1408]),
                       reads=[], writes=[wst_r], dma=True)
                for cg in range(3):
                    pp, pp_r, wid = pdst[cg]
                    w0 = cg * 512
                    sch.op("pe", lambda k=k, pp=pp, w0=w0, wid=wid:
                           nc.tensor.matmul(pp[0:1, 0:wid], lnbc[:, k:k + 1],
                                            wst[:, w0:w0 + wid],
                                            start=(k == 0), stop=(k == 7)),
                           reads=[lnbc_r, wst_r], writes=[pp_r])
                sch.op("dve", lambda k=k, base=base: nc.vector.tensor_scalar_mul(
                    wbig[:, base + k * 1408:base + (k + 1) * 1408],
                    wst[:, 0:1408], lngc[:, k:k + 1]),
                       reads=[wst_r, lngc_r], writes=[wbig_r])
            bd = b1ad if which == 0 else b1gd
            dst = beffa if which == 0 else beffg
            dst_r = beffa_r if which == 0 else beffg_r
            sch.op("sp", lambda bd=bd: nc.sync.dma_start(out=rowst[:, 0:1408],
                                                         in_=bd[:, hc0:hc0 + 1408]),
                   writes=[rowst_r], dma=True)
            for cg in range(3):
                pp, pp_r, wid = pdst[cg]
                w0 = cg * 512
                sch.op("dve", lambda pp=pp, w0=w0, wid=wid: nc.vector.tensor_add(
                    rowst[0:1, 1408:1408 + wid], pp[0:1, 0:wid], rowst[0:1, w0:w0 + wid]),
                       reads=[pp_r, rowst_r], writes=[rowst_r])
                sch.op("act", lambda dst=dst, w0=w0, wid=wid: nc.scalar.copy(
                    dst[:, w0:w0 + wid], rowst[0:1, 1408:1408 + wid]),
                       reads=[rowst_r], writes=[dst_r])
        for k in range(11):
            sch.op("sp", lambda k=k: nc.sync.dma_start(out=wst[:, 0:D],
                                                       in_=w2d[hc0 + 128 * k:hc0 + 128 * (k + 1), :]),
                   reads=[], writes=[wst_r], dma=True)
            sch.op("act", lambda k=k: nc.scalar.copy(wbig[:, 22528 + k * D:22528 + (k + 1) * D], wst[:, 0:D]),
                   reads=[wst_r], writes=[wbig_r])

    def ffn_half(half):
        for tp in range(32):
            i = tp % 2
            r0 = 641 + 128 * tp
            sch.op("sp", lambda r0=r0, i=i: nc.sync.dma_start(out=hs[i][:, 0:512],
                                                              in_=mscr[r0:r0 + 128, 0:512]),
                   reads=[mscr_r], writes=[hs_r[i]], dma=True)
            sch.op("sp", lambda r0=r0, i=i: nc.sync.dma_start(out=hs[i][:, 512:D],
                                                              in_=mscr[r0 - 1:r0 + 127, 512:D]),
                   reads=[mscr_r], writes=[hs_r[i]], dma=True)
            emit_ln(sch, nc, scr[i], hs_r[i], hs[i][:, :], h2n_r[i], h2n[i][:, :])
            emit_transp8(sch, nc, pT, pT_r, ident, ident_r, h2n_r[i],
                         lambda p, i=i: h2n[i][:, p * 128:(p + 1) * 128],
                         hfm_r[i],
                         lambda p, i=i: hfm[i][:, p * 128:(p + 1) * 128])
            for cg in range(3):
                w0 = cg * 512
                wid = 512 if cg < 2 else 384
                for k in range(8):
                    sch.op("pe", lambda i=i, k=k, w0=w0, wid=wid:
                           nc.tensor.matmul(p512a[:, 0:wid], hfm[i][:, k * 128:(k + 1) * 128],
                                            wbig[:, k * 1408 + w0:k * 1408 + w0 + wid],
                                            start=(k == 0), stop=False),
                           reads=[hfm_r[i], wbig_r], writes=[p512a_r])
                sch.op("pe", lambda w0=w0, wid=wid: nc.tensor.matmul(p512a[:, 0:wid], ones1bf[:, :],
                                                                    beffa[:, w0:w0 + wid],
                                                                    start=False, stop=True),
                       reads=[ones1bf_r, beffa_r], writes=[p512a_r])
                for k in range(8):
                    sch.op("pe", lambda i=i, k=k, w0=w0, wid=wid:
                           nc.tensor.matmul(p512b[:, 0:wid], hfm[i][:, k * 128:(k + 1) * 128],
                                            wbig[:, 11264 + k * 1408 + w0:11264 + k * 1408 + w0 + wid],
                                            start=(k == 0), stop=False),
                           reads=[hfm_r[i], wbig_r], writes=[p512b_r])
                sch.op("pe", lambda w0=w0, wid=wid: nc.tensor.matmul(p512b[:, 0:wid], ones1bf[:, :],
                                                                    beffg[:, w0:w0 + wid],
                                                                    start=False, stop=True),
                       reads=[ones1bf_r, beffg_r], writes=[p512b_r])
                sch.op("act", lambda wid=wid: nc.scalar.activation(ggb[:, 0:wid], p512b[:, 0:wid], AF.Gelu),
                       reads=[p512b_r], writes=[ggb_r])
                sch.op("dve", lambda wid=wid: nc.vector.tensor_mul(prod[:, 0:wid], p512a[:, 0:wid], ggb[:, 0:wid]),
                       reads=[p512a_r, ggb_r], writes=[prod_r])
                nk = wid // 128
                for kk in range(nk):
                    sch.op("pe", lambda kk=kk: nc.tensor.transpose(pT[0][:, :],
                                                                   prod[:, kk * 128:(kk + 1) * 128],
                                                                   ident[:, :]),
                           reads=[prod_r, ident_r], writes=[pT_r[0]])
                    sch.op("act", lambda kk=kk, w0=w0: nc.scalar.copy(hidT[:, w0 + kk * 128:w0 + (kk + 1) * 128],
                                                                      pT[0][:, :]),
                           reads=[pT_r[0]], writes=[hidT_r])
            for cg in range(2):
                for k in range(11):
                    sch.op("pe", lambda k=k, cg=cg: nc.tensor.matmul(pm[:, cg * 512:(cg + 1) * 512],
                                                                     hidT[:, k * 128:(k + 1) * 128],
                                                                     wbig[:, 22528 + k * D + cg * 512:22528 + k * D + cg * 512 + 512],
                                                                     start=(k == 0), stop=(k == 10)),
                           reads=[hidT_r, wbig_r], writes=[pm_r])
            if half == 0:
                sch.op("sp", lambda r0=r0, i=i: nc.sync.dma_start(out=xrs[i][:, :], in_=mscr[r0:r0 + 128, :]),
                       reads=[mscr_r], writes=[xrs_r[i]], dma=True)
                sch.op("dve", lambda i=i: nc.vector.tensor_add(yps[i][:, :], pm[:, :], xrs[i][:, :]),
                       reads=[pm_r, xrs_r[i]], writes=[yps_r[i]])
                sch.op("pool", lambda tp=tp, i=i: nc.gpsimd.dma_start(out=ypart[128 * tp:128 * (tp + 1), :],
                                                                      in_=yps[i][:, :]),
                       reads=[yps_r[i]], writes=[ypart_r], dma=True)
            else:
                sch.op("sp", lambda tp=tp, i=i: nc.sync.dma_start(out=xrs[i][:, :],
                                                                  in_=ypart[128 * tp:128 * (tp + 1), :]),
                       reads=[ypart_r], writes=[xrs_r[i]], dma=True)
                sch.op("dve", lambda i=i: nc.vector.tensor_add(yps[i][:, :], pm[:, :], xrs[i][:, :]),
                       reads=[pm_r, xrs_r[i]], writes=[yps_r[i]])
                sch.op("dve", lambda i=i: nc.vector.tensor_add(xrs[i][:, :], yps[i][:, :], b2bc[:, :]),
                       reads=[yps_r[i], b2bc_r], writes=[xrs_r[i]])
                if final_add:
                    sch.op("sp", lambda tp=tp, i=i: nc.sync.dma_start(
                        out=hs[i][:, :], in_=xresd[xres_off + 128 * tp:xres_off + 128 * (tp + 1), :]),
                           writes=[hs_r[i]], dma=True)
                    sch.op("dve", lambda i=i: nc.vector.tensor_add(yps[i][:, :], xrs[i][:, :], hs[i][:, :]),
                           reads=[xrs_r[i], hs_r[i]], writes=[yps_r[i]])
                    sch.op("pool", lambda tp=tp, i=i: nc.gpsimd.dma_start(
                        out=yo_ap[out_off + 128 * tp:out_off + 128 * (tp + 1), :], in_=yps[i][:, :]),
                           reads=[yps_r[i]], writes=[out_r], dma=True)
                else:
                    sch.op("pool", lambda tp=tp, i=i: nc.gpsimd.dma_start(
                        out=yo_ap[out_off + 128 * tp:out_off + 128 * (tp + 1), :], in_=xrs[i][:, :]),
                           reads=[xrs_r[i]], writes=[out_r], dma=True)

    for half in range(2):
        load_half(half)
        ffn_half(half)

    es.close()



def emit_C(nc, sch, dr, xin_ap, xin_off, pref="C"):
    wqkvd = dr["bwqkvd"]
    idb = dr["idb"]
    oncb = dr["oncb"]
    parto = dr["ccr_in"]
    qouto = dr["qscr"]

    es = ExitStack()
    S = lambda n, sh, dt: es.enter_context(nc.sbuf_tensor(pref + n, sh, dt))
    P = lambda n, sh, dt: es.enter_context(nc.psum_tensor(pref + n, sh, dt))

    wb = S("wb", [128, 8 * 3072], BF16)
    wst = S("wstc", [128, 3072], F32)
    ident = S("identc", [128, 128], BF16)
    onec = S("onec", [128, 1], BF16)
    xt = [S("xtc%d" % i, [128, D], F32) for i in range(2)]
    xnb = [S("xnbc%d" % i, [128, D], BF16) for i in range(2)]
    xfm = [S("xfmc%d" % i, [128, D], BF16) for i in range(2)]
    qsb = [S("qsbc%d" % i, [128, D], BF16) for i in range(2)]
    eksb = [S("eksbc%d" % i, [128, D], BF16) for i in range(2)]
    vsb = [S("vsbc%d" % i, [128, D], BF16) for i in range(2)]
    csb = S("csb", [64, 1088], F32)
    scr = ln_scratch(S, "c")

    pg = P("pg", [128, 512], F32)
    pT = [P("pTc%d" % i, [128, 128], BF16) for i in range(2)]
    pctx = P("pctx", [64, 1088], F32)

    wb_r, wst_r, ident_r, onec_r, csb_r = R(), R(), R(), R(), R()
    xt_r = [R(), R()]
    xnb_r = [R(), R()]
    xfm_r = [R(), R()]
    qsb_r = [R(), R()]
    eksb_r = [R(), R()]
    vsb_r = [R(), R()]
    pg_r, pctx_r = R(), R()
    pT_r = [R(), R()]
    out_r = R()

    sch.op("sp", lambda: nc.sync.dma_start(out=ident[:, :], in_=idb[:, :]), writes=[ident_r], dma=True)
    sch.op("sp", lambda: nc.sync.dma_start(out=onec[:, :], in_=oncb[:, :]), writes=[onec_r], dma=True)
    for k in range(8):
        emit_wload(sch, nc, wst, wst_r, wqkvd, 128 * k, 3072, wb[:, k * 3072:(k + 1) * 3072], wb_r)

    for t in range(32):
        i = t % 2
        sch.op("sp", lambda t=t, i=i: nc.sync.dma_start(
            out=xt[i][:, :], in_=xin_ap[xin_off + 128 * t:xin_off + 128 * (t + 1), :]),
               writes=[xt_r[i]], dma=True)
        emit_ln(sch, nc, scr[i], xt_r[i], xt[i][:, :], xnb_r[i], xnb[i][:, :])
        emit_transp8(sch, nc, pT, pT_r, ident, ident_r, xnb_r[i],
                     lambda p, i=i: xnb[i][:, p * 128:(p + 1) * 128],
                     xfm_r[i],
                     lambda p, i=i: xfm[i][:, p * 128:(p + 1) * 128])
        for part in range(3):
            for cg in range(2):
                for k in range(8):
                    sch.op("pe", lambda i=i, part=part, cg=cg, k=k:
                           nc.tensor.matmul(pg[:, :], xfm[i][:, k * 128:(k + 1) * 128],
                                            wb[:, k * 3072 + part * 1024 + cg * 512:k * 3072 + part * 1024 + cg * 512 + 512],
                                            start=(k == 0), stop=(k == 7)),
                           reads=[xfm_r[i], wb_r], writes=[pg_r])
                if part == 0:
                    sch.op("act", lambda i=i, cg=cg: nc.scalar.copy(qsb[i][:, cg * 512:(cg + 1) * 512], pg[:, :]),
                           reads=[pg_r], writes=[qsb_r[i]])
                elif part == 1:
                    sch.op("act", lambda i=i, cg=cg: nc.scalar.activation(eksb[i][:, cg * 512:(cg + 1) * 512],
                                                                          pg[:, :], AF.Exp),
                           reads=[pg_r], writes=[eksb_r[i]])
                else:
                    sch.op("act", lambda i=i, cg=cg: nc.scalar.copy(vsb[i][:, cg * 512:(cg + 1) * 512], pg[:, :]),
                           reads=[pg_r], writes=[vsb_r[i]])
        sch.op("pool", lambda t=t, i=i: nc.gpsimd.dma_start(out=qouto[128 * t:128 * (t + 1), :], in_=qsb[i][:, :]),
               reads=[qsb_r[i]], writes=[out_r], dma=True)
        for h in range(16):
            sch.op("pe", lambda i=i, h=h, t=t: nc.tensor.matmul(pctx[0:64, h * 68:h * 68 + 64],
                                                                eksb[i][:, h * 64:(h + 1) * 64],
                                                                vsb[i][:, h * 64:(h + 1) * 64],
                                                                start=(t == 0), stop=(t == 31)),
                   reads=[eksb_r[i], vsb_r[i]], writes=[pctx_r])
            sch.op("pe", lambda i=i, h=h, t=t: nc.tensor.matmul(pctx[0:64, h * 68 + 64:h * 68 + 65],
                                                                eksb[i][:, h * 64:(h + 1) * 64],
                                                                onec[:, :],
                                                                start=(t == 0), stop=(t == 31)),
                   reads=[eksb_r[i], onec_r], writes=[pctx_r])
    sch.op("act", lambda: nc.scalar.copy(csb[:, :], pctx[0:64, :]), reads=[pctx_r], writes=[csb_r])
    sch.op("pool", lambda: nc.gpsimd.dma_start(out=parto[:, :], in_=csb[:, :]),
           reads=[csb_r], writes=[out_r], dma=True)

    es.close()



def emit_D(nc, sch, dr, xd_ap, xd_off, out_ap, out_off, pref="E"):
    parts = dr["ccr_out"]
    qind = dr["qscr"]
    woutd = dr["bwoutd"]
    boutd = dr["bboutd"]
    idb = dr["idb"]
    on1f = dr["on1f"]

    es = ExitStack()
    S = lambda n, sh, dt: es.enter_context(nc.sbuf_tensor(pref + n, sh, dt))
    P = lambda n, sh, dt: es.enter_context(nc.psum_tensor(pref + n, sh, dt))

    wb = S("wbd", [128, 8 * D], BF16)
    wst = S("wstd", [128, D], F32)
    ident = S("identd", [128, 128], BF16)
    ones1 = S("ones1d", [1, 128], F32)
    csum = S("csumd", [64, 1088], F32)
    crc = S("crcd", [64, 16], F32)
    ctxb = S("ctxbd", [128, D], BF16)
    boutbc = S("boutbc", [128, D], F32)
    browst = S("browst", [1, D], F32)
    qt = [S("qtd%d" % i, [128, D], BF16) for i in range(2)]
    qfm = [S("qfmd%d" % i, [128, D], BF16) for i in range(2)]
    ofb = [S("ofbd%d" % i, [128, D], BF16) for i in range(2)]
    xr = [S("xrd%d" % i, [128, D], F32) for i in range(2)]
    ysb = [S("ysbd%d" % i, [128, D], F32) for i in range(2)]

    pT = [P("pTd%d" % i, [128, 128], BF16) for i in range(2)]
    pof = P("pofd", [128, D], F32)
    py = P("pyd", [128, D], F32)
    pbb = P("pbbd", [128, 512], F32)

    wb_r, wst_r, ident_r, ones1_r = R(), R(), R(), R()
    csum_r, crc_r, ctxb_r, boutbc_r, browst_r = R(), R(), R(), R(), R()
    qt_r = [R(), R()]
    qfm_r = [R(), R()]
    ofb_r = [R(), R()]
    xr_r = [R(), R()]
    ysb_r = [R(), R()]
    pT_r = [R(), R()]
    pof_r, py_r, pbb_r = R(), R(), R()
    out_r = R()

    sch.op("sp", lambda: nc.sync.dma_start(out=ident[:, :], in_=idb[:, :]), writes=[ident_r], dma=True)
    sch.op("sp", lambda: nc.sync.dma_start(out=ones1[:, :], in_=on1f[:, :]), writes=[ones1_r], dma=True)
    sch.op("sp", lambda: nc.sync.dma_start(out=browst[:, :], in_=boutd[:, :]), writes=[browst_r], dma=True)
    sch.op("sp", lambda: nc.sync.dma_start(out=csum[:, :], in_=parts[0:64, :]),
           reads=[dr["ccr_out_r"]], writes=[csum_r], dma=True)
    for k in range(8):
        emit_wload(sch, nc, wst, wst_r, woutd, 128 * k, D, wb[:, k * D:(k + 1) * D], wb_r)
    for cg in range(2):
        sch.op("pe", lambda cg=cg: nc.tensor.matmul(pbb[:, :], ones1[:, :],
                                                    browst[:, cg * 512:(cg + 1) * 512],
                                                    start=True, stop=True),
               reads=[ones1_r, browst_r], writes=[pbb_r])
        sch.op("act", lambda cg=cg: nc.scalar.copy(boutbc[:, cg * 512:(cg + 1) * 512], pbb[:, :]),
               reads=[pbb_r], writes=[boutbc_r])
    for h in range(16):
        sch.op("dve", lambda h=h: nc.vector.reciprocal(crc[:, h:h + 1], csum[:, h * 68 + 64:h * 68 + 65]),
               reads=[csum_r], writes=[crc_r])
        sch.op("dve", lambda h=h: nc.vector.tensor_scalar_mul(ctxb[0:64, h * 64:(h + 1) * 64],
                                                              csum[:, h * 68:h * 68 + 64],
                                                              crc[:, h:h + 1]),
               reads=[csum_r, crc_r], writes=[ctxb_r])
    sch.op("pool", lambda: nc.gpsimd.dma_start(out=ctxb[64:128, :], in_=ctxb[0:64, :]),
           reads=[ctxb_r], writes=[ctxb_r], dma=True)
    for t in range(32):
        i = t % 2
        sch.op("sp", lambda t=t, i=i: nc.sync.dma_start(out=qt[i][:, :], in_=qind[128 * t:128 * (t + 1), :]),
               writes=[qt_r[i]], dma=True)
        emit_transp8(sch, nc, pT, pT_r, ident, ident_r, qt_r[i],
                     lambda p, i=i: qt[i][:, p * 128:(p + 1) * 128],
                     qfm_r[i],
                     lambda p, i=i: qfm[i][:, p * 128:(p + 1) * 128])
        for h in range(16):
            hp, dh = h // 2, h % 2
            sch.op("pe", lambda i=i, h=h, hp=hp, dh=dh:
                   nc.tensor.matmul(pof[64 * dh:64 * dh + 64, hp * 128:(hp + 1) * 128],
                                    ctxb[64 * dh:64 * dh + 64, h * 64:(h + 1) * 64],
                                    qfm[i][64 * dh:64 * dh + 64, hp * 128:(hp + 1) * 128],
                                    start=True, stop=True),
                   reads=[ctxb_r, qfm_r[i]], writes=[pof_r])
        sch.op("act", lambda i=i: nc.scalar.copy(ofb[i][:, :], pof[:, :]),
               reads=[pof_r], writes=[ofb_r[i]])
        sch.op("sp", lambda t=t, i=i: nc.sync.dma_start(
            out=xr[i][:, :], in_=xd_ap[xd_off + 128 * t:xd_off + 128 * (t + 1), :]),
               writes=[xr_r[i]], dma=True)
        for cg in range(2):
            for k in range(8):
                sch.op("pe", lambda i=i, cg=cg, k=k:
                       nc.tensor.matmul(py[:, cg * 512:(cg + 1) * 512],
                                        ofb[i][:, k * 128:(k + 1) * 128],
                                        wb[:, k * D + cg * 512:k * D + cg * 512 + 512],
                                        start=(k == 0), stop=(k == 7)),
                       reads=[ofb_r[i], wb_r], writes=[py_r])
        sch.op("dve", lambda i=i: nc.vector.tensor_add(ysb[i][:, :], py[:, :], xr[i][:, :]),
               reads=[py_r, xr_r[i]], writes=[ysb_r[i]])
        sch.op("dve", lambda i=i: nc.vector.tensor_add(xr[i][:, :], ysb[i][:, :], boutbc[:, :]),
               reads=[ysb_r[i], boutbc_r], writes=[xr_r[i]])
        sch.op("pool", lambda t=t, i=i: nc.gpsimd.dma_start(
            out=out_ap[out_off + 128 * t:out_off + 128 * (t + 1), :], in_=xr[i][:, :]),
               reads=[xr_r[i]], writes=[out_r], dma=True)

    es.close()



RG = [[0, 1, 2, 3], [4, 5, 6, 7]]


def emit_exchange(nc, sch, dr, tgt, pref):
    ccin = dr["ccin"]
    ccout = dr["ccout"]
    ccin_r, ccout_r = R(), R()
    es = ExitStack()
    S = lambda n, sh, dt: es.enter_context(nc.sbuf_tensor(pref + n, sh, dt))
    sl = [S("sl%d" % r, [128, D], F32) for r in range(4)]
    acc = [S("acc%d" % i, [128, D], F32) for i in range(2)]
    sel = S("sel", [128, 4], F32)
    sl_r = [R() for _ in range(4)]
    acc_r = [R(), R()]
    sel_r = R()
    tgt_r = R()
    sch.op("sp", lambda: nc.sync.dma_start(out=sel[:, :], in_=dr["selsb"][:, :]),
           writes=[sel_r], dma=True)
    sch.op("pool", lambda: nc.gpsimd.dma_start(out=ccin[:, :], in_=tgt[4096:4737, :]),
           writes=[ccin_r], dma=True)
    sch.op("pool", lambda: nc.gpsimd.collective_compute(
        "AllGather", ALU.bypass, replica_groups=RG, ins=[ccin.opt()], outs=[ccout.opt()]),
           reads=[ccin_r], writes=[ccout_r])
    for tt in range(6):
        r0 = 128 * tt
        Pn = 128 if tt < 5 else 1
        for r in range(4):
            sch.op("sp", lambda r=r, r0=r0, Pn=Pn: nc.sync.dma_start(
                out=sl[r][0:Pn, :], in_=ccout[641 * r + r0:641 * r + r0 + Pn, :]),
                   reads=[ccout_r], writes=[sl_r[r]], dma=True)
        sch.op("dve", lambda r0=r0, Pn=Pn: nc.vector.tensor_scalar_mul(
            acc[0][0:Pn, :], sl[0][0:Pn, :], sel[0:Pn, 0:1]),
               reads=[sl_r[0], sel_r], writes=[acc_r[0]])
        for r in range(1, 4):
            a, b = (r - 1) % 2, r % 2
            sch.op("dve", lambda r=r, a=a, b=b, Pn=Pn: nc.vector.scalar_tensor_tensor(
                acc[b][0:Pn, :], sl[r][0:Pn, :], sel[0:Pn, r:r + 1], acc[a][0:Pn, :],
                ALU.mult, ALU.add),
                   reads=[sl_r[r], sel_r, acc_r[a]], writes=[acc_r[b]])
        sch.op("pool", lambda r0=r0, Pn=Pn: nc.gpsimd.dma_start(
            out=tgt[r0:r0 + Pn, :], in_=acc[1][0:Pn, :]),
               reads=[acc_r[1]], writes=[tgt_r], dma=True)
    es.close()


def build_ALL():
    nc = bass.Bass("TRN2", target_bir_lowering=False)
    dd = lambda n, sh, dt, kind="ExternalInput": nc.dram_tensor(n, sh, dt, kind=kind).ap()
    dr = {}
    dr["xa"] = dd("xa", [4224, D], F32)
    dr["aw1d"] = dd("aw1d", [D, D], F32)
    dr["aw2d"] = dd("aw2d", [D, D], F32)
    dr["abiasr"] = dd("abiasr", [1, D], F32)
    dr["etT"] = dd("etT", [D, 21], F32)
    dr["idb"] = dd("idb", [128, 128], BF16)
    dr["on1f"] = dd("on1f", [1, 128], F32)
    dr["on1b"] = dd("on1b", [1, 128], BF16)
    dr["oncb"] = dd("oncb", [128, 1], BF16)
    dr["bmd"] = dd("bmd", [128, 640], F32)
    dr["kvd"] = dd("kvd", [1, 4736], BF16)
    dr["selsb"] = dd("selsb", [128, 4], F32)
    for n in ("cq8", "sq8", "ck8", "sk8"):
        dr[n] = dd(n, [4736, 512], F32)
    lws = []
    for l in range(2):
        lw = {}
        p = "L%d" % l
        lw["wqkvd"] = dd(p + "wqkvd", [D, 1536], F32)
        lw["wod"] = dd(p + "wod", [512, D], F32)
        lw["w1ad"] = dd(p + "w1ad", [D, 2816], F32)
        lw["w1gd"] = dd(p + "w1gd", [D, 2816], F32)
        lw["w2d"] = dd(p + "w2d", [2816, D], F32)
        lw["b1ad"] = dd(p + "b1ad", [1, 2816], F32)
        lw["b1gd"] = dd(p + "b1gd", [1, 2816], F32)
        lw["b2d"] = dd(p + "b2d", [1, D], F32)
        lw["lngd"] = dd(p + "lngd", [D, 1], F32)
        lw["lnbd"] = dd(p + "lnbd", [D, 1], F32)
        lw["qs8d"] = dd(p + "qs8d", [1, 512], F32)
        lw["ks8d"] = dd(p + "ks8d", [1, 512], F32)
        lws.append(lw)
    dr["bwqkvd"] = dd("bwqkvd", [D, 3072], F32)
    dr["bwoutd"] = dd("bwoutd", [D, D], F32)
    dr["bboutd"] = dd("bboutd", [1, D], F32)
    yo = dd("yo", [CH, D], F32, "ExternalOutput")
    xb1a = dd("xb1a", [4737, D], F32, "Internal")
    xb1b = dd("xb1b", [4737, D], F32, "Internal")
    xb2a = dd("xb2a", [4737, D], F32, "Internal")
    xb2b = dd("xb2b", [4737, D], F32, "Internal")
    xbx2 = dd("xbx2", [CH, D], F32, "Internal")
    dr["mscr"] = dd("mscr", [4737, D], F32, "Internal")
    dr["ypart"] = dd("ypart", [CH, D], F32, "Internal")
    dr["uscr"] = dd("uscr", [4737, D], F32, "Internal")
    dr["qscr"] = dd("qscr", [CH, D], BF16, "Internal")
    dr["ccin"] = dd("ccin", [641, D], F32, "Internal")
    dr["ccout"] = dd("ccout", [4 * 641, D], F32, "Internal")
    dr["ccr_in"] = dd("ccr_in", [64, 1088], F32, "Internal")
    dr["ccr_out"] = dd("ccr_out", [64, 1088], F32, "Internal")
    dr["ccr_out_r"] = R()

    es = ExitStack()
    sems = mk_sems(nc, es)
    sch = Sched()

    emit_A(nc, sch, dr, xb1a, 641, "A")
    sch.barrier()
    emit_exchange(nc, sch, dr, xb1a, "X0")
    sch.barrier()
    emit_B(nc, sch, dr, lws[0], True, False, xb1a, xb1b, 641, pref="B0")
    sch.barrier()
    emit_exchange(nc, sch, dr, xb1b, "X1")
    sch.barrier()
    emit_B(nc, sch, dr, lws[1], False, True, xb1b, xbx2, 0, xres=xb1a, xres_off=641, pref="B1")
    sch.barrier()
    emit_C(nc, sch, dr, xbx2, 0, "C")
    sch.barrier()
    sch.op("pool", lambda: nc.gpsimd.collective_compute(
        "AllReduce", ALU.add, replica_groups=RG, ins=[dr["ccr_in"].opt()], outs=[dr["ccr_out"].opt()]),
           writes=[dr["ccr_out_r"]])
    sch.barrier()
    emit_D(nc, sch, dr, xbx2, 0, xb2a, 641, "E")
    sch.barrier()
    emit_exchange(nc, sch, dr, xb2a, "X2")
    sch.barrier()
    emit_B(nc, sch, dr, lws[0], True, False, xb2a, xb2b, 641, pref="B2")
    sch.barrier()
    emit_exchange(nc, sch, dr, xb2b, "X3")
    sch.barrier()
    emit_B(nc, sch, dr, lws[1], False, True, xb2b, yo, 0, xres=xb2a, xres_off=641, pref="B3")

    sch.replay(nc, sems)
    es.close()
    return nc



def _rows(Xb, lo, hi):
    out = np.zeros((hi - lo, D), np.float32)
    a = max(lo, 0)
    if a < hi:
        out[a - lo:] = Xb[a:hi]
    return np.ascontiguousarray(out)


_CONST = {}


def _consts():
    if _CONST:
        return _CONST
    n, DH, W = 16384, 64, 512
    hi = np.arange(0, DH, 2, dtype=np.float64)
    inv = 1.0 / (10000.0 ** (hi / DH))
    t = np.arange(n, dtype=np.float64)
    fr = t[:, None] * inv[None, :]
    fr2 = np.concatenate([fr, fr], 1)
    base = (hi + 0.4 * DH) / (1.4 * DH)
    power = (t - n // 2) / (W // 2)
    sc = base[None, :] ** power[:, None]
    sc2 = np.concatenate([sc, sc], 1)
    cos, sin = np.cos(fr2), np.sin(fr2)
    sgn = np.concatenate([-np.ones(32), np.ones(32)])[None, :]
    tile8 = lambda M: np.ascontiguousarray(np.tile(M.astype(np.float32), (1, 8)))
    _CONST["CQ8"] = tile8(cos * sc2 * 8.0)
    _CONST["SQ8"] = tile8(sgn * sin * sc2 * 8.0)
    _CONST["CK8"] = tile8(cos / sc2)
    _CONST["SK8"] = tile8(sgn * sin / sc2)
    r = np.arange(128)[:, None]
    c = np.arange(640)[None, :]
    m = np.where((c >= r) & (c <= r + 512), 0.0, NEG).astype(np.float32)
    _CONST["bm"] = np.ascontiguousarray(m)
    _CONST["idb"] = np.eye(128, dtype=BF)
    _CONST["on1f"] = np.ones((1, 128), np.float32)
    _CONST["on1b"] = np.ones((1, 128), BF)
    _CONST["oncb"] = np.ones((128, 1), BF)
    return _CONST


def _pad_cols(a, w):
    out = np.zeros((a.shape[0], w), np.float32)
    out[:, :a.shape[1]] = a
    return np.ascontiguousarray(out)


def kernel(**inp):
    C = _consts()
    nc = build_ALL()
    X0 = np.asarray(inp["L0_attn_a_input_data"], np.float32)
    et = np.asarray(inp["L0_attn_a_embeddings_table"], np.float32)

    shared = {
        "aw1d": np.ascontiguousarray(inp["L0_attn_a_W1"], np.float32),
        "aw2d": np.ascontiguousarray(inp["L0_attn_a_W2"], np.float32),
        "abiasr": np.ascontiguousarray(inp["L0_attn_a_bias"], np.float32).reshape(1, D),
        "etT": np.ascontiguousarray(et.T),
        "idb": C["idb"], "on1f": C["on1f"], "on1b": C["on1b"], "oncb": C["oncb"],
        "bmd": C["bm"],
        "bwqkvd": np.ascontiguousarray(inp["L0_attn_b_W_qkv"], np.float32),
        "bwoutd": np.ascontiguousarray(inp["L0_attn_b_W_out"], np.float32),
        "bboutd": np.ascontiguousarray(inp["L0_attn_b_b_out"], np.float32).reshape(1, D),
    }
    for l in range(2):
        p = "L%d" % l
        w1 = np.asarray(inp["L0_ffn_w1"], np.float32)[l]
        b1 = np.asarray(inp["L0_ffn_b1"], np.float32)[l]
        w2 = np.zeros((2816, D), np.float32)
        w2[:2730] = np.asarray(inp["L0_ffn_w2"], np.float32)[l]
        qs = np.asarray(inp["L0_ffn_q_scale"], np.float32)[l]
        ks = np.asarray(inp["L0_ffn_k_scale"], np.float32)[l]
        shared.update({
            p + "wqkvd": np.ascontiguousarray(np.asarray(inp["L0_ffn_wqkv"], np.float32)[l]),
            p + "wod": np.ascontiguousarray(np.asarray(inp["L0_ffn_wo"], np.float32)[l]),
            p + "w1ad": _pad_cols(w1[:, :2730], 2816),
            p + "w1gd": _pad_cols(w1[:, 2730:], 2816),
            p + "w2d": w2,
            p + "b1ad": _pad_cols(b1[:2730].reshape(1, -1), 2816),
            p + "b1gd": _pad_cols(b1[2730:].reshape(1, -1), 2816),
            p + "b2d": np.ascontiguousarray(np.asarray(inp["L0_ffn_b2"], np.float32)[l].reshape(1, D)),
            p + "lngd": np.ascontiguousarray(np.asarray(inp["L0_ffn_ln_g"], np.float32)[l].reshape(D, 1)),
            p + "lnbd": np.ascontiguousarray(np.asarray(inp["L0_ffn_ln_b"], np.float32)[l].reshape(D, 1)),
            p + "qs8d": np.ascontiguousarray(np.tile(qs, 8).reshape(1, 512)),
            p + "ks8d": np.ascontiguousarray(np.tile(ks, 8).reshape(1, 512)),
        })

    maps = []
    for c in range(8):
        b, q = c // 4, c % 4
        T0 = q * CH
        kv = np.zeros((1, 4736), np.float32)
        if q == 0:
            kv[0, :LEAD] = NEG
        sel = np.zeros((128, 4), np.float32)
        if q > 0:
            sel[:, q - 1] = 1.0
        mm = dict(shared)
        mm.update({
            "xa": _rows(X0[b], T0 - 128, T0 + CH),
            "kvd": kv.astype(BF),
            "selsb": sel,
            "cq8": _tab(C["CQ8"], T0),
            "sq8": _tab(C["SQ8"], T0),
            "ck8": _tab(C["CK8"], T0),
            "sk8": _tab(C["SK8"], T0),
        })
        maps.append(mm)

    res = run_bass_kernel_spmd(nc, maps, core_ids=list(range(8)))
    X = np.empty((2, 16384, D), np.float32)
    for c in range(8):
        b, q = c // 4, c % 4
        X[b, q * CH:(q + 1) * CH] = np.asarray(res.results[c]["yo"], np.float32)
    return X


def _tab(M, T0):
    lo = T0 - LEAD
    out = np.zeros((4736, 512), np.float32)
    a = max(lo, 0)
    out[a - lo:] = M[a:T0 + CH]
    return np.ascontiguousarray(out)
